# version 01
# baseline (speedup 1.0000x reference)
import math
import contextlib
import numpy as np
import ml_dtypes
import concourse.bass as bass
import concourse.mybir as mybir
from concourse.bass_utils import run_bass_kernel_spmd

F32 = mybir.dt.float32
BF16 = mybir.dt.bfloat16
AF = mybir.ActivationFunctionType
ALU = mybir.AluOpType
AX = mybir.AxisListType

D = 1024
DK = 512
NH = 4
DH = 128
DV = 256
CH = 64
RG = 16
IM = 2816
NE = 8
EH = 2816
TC = 2048
TB = 1024
NT = TB // 128
NCHUNK = TB // CH
GLN = 16.0
LNSCALE = math.log(DH ** -0.5)

_DEBUG_TAPS = []
_DEBUG_SHAPES = {}
_DEBUG_OUT = None
_PHASES = 99
_SPH = 99
_GSTEP = 99

GUARD_DIST = 2


class _Op:
    __slots__ = ("engine", "fn", "reads", "writes", "dma", "tag", "gseq",
                 "eseq", "deps", "need_inc", "inc_val", "snapshot",
                 "final_waits")

    def __init__(self, engine, fn, reads, writes, dma, tag):
        self.engine = engine
        self.fn = fn
        self.reads = reads
        self.writes = writes
        self.dma = dma
        self.tag = tag
        self.deps = []
        self.need_inc = False
        self.inc_val = None
        self.snapshot = None
        self.final_waits = None


class Emitter:
    def __init__(self):
        self.ops = []
        self.eng_ops = {e: [] for e in ("pe", "act", "dve", "gp", "sq")}
        self.arenas = {}

    def _arena(self, name):
        if name not in self.arenas:
            self.arenas[name] = {"w": [], "r": []}
        return self.arenas[name]

    @staticmethod
    def _split(entries, lo, hi):
        out, hit = [], []
        for (a, b, op) in entries:
            if b <= lo or a >= hi:
                out.append((a, b, op))
                continue
            hit.append(op)
            if a < lo:
                out.append((a, lo, op))
            if b > hi:
                out.append((hi, b, op))
        return out, hit

    @staticmethod
    def _overlapping(entries, lo, hi):
        return [op for (a, b, op) in entries if not (b <= lo or a >= hi)]

    def op(self, engine, fn, reads=(), writes=(), dma=False, tag=""):
        reads = list(reads)
        writes = list(writes)
        psum_rd = [r for r in reads if r[0].startswith("pb")]
        if psum_rd:
            reads = [r for r in reads if not r[0].startswith("pb")]
            writes = writes + psum_rd
        o = _Op(engine, fn, list(reads), list(writes), dma, tag)
        if dma:
            o.need_inc = True
        o.gseq = len(self.ops)
        o.eseq = len(self.eng_ops[engine])
        self.eng_ops[engine].append(o)
        self.ops.append(o)

        deps = set()
        for (ar, lo, hi) in o.reads:
            st = self._arena(ar)
            deps.update(self._overlapping(st["w"], lo, hi))
            st["r"].append((lo, hi, o))
        for (ar, lo, hi) in o.writes:
            st = self._arena(ar)
            st["w"], whit = self._split(st["w"], lo, hi)
            deps.update(whit)
            st["r"], rhit = self._split(st["r"], lo, hi)
            deps.update(rhit)
            st["w"].append((lo, hi, o))
        deps.discard(o)

        for d in deps:
            if d.engine == o.engine and not d.dma and not o.dma:
                if o.engine == "pe":
                    continue
                if (o.eseq - d.eseq) >= GUARD_DIST:
                    continue
            o.deps.append(d)
            d.need_inc = True
        return o

    def sem_key(self, op):
        return (op.engine, "d" if op.dma else "c")

    def finalize(self):
        counts = {}
        for o in self.ops:
            if o.need_inc:
                k = self.sem_key(o)
                counts[k] = counts.get(k, 0) + (16 if o.dma else 1)
                o.inc_val = counts[k]
        eng_clock = {}
        for o in self.ops:
            clk = eng_clock.setdefault(o.engine, {})
            fw = []
            for d in sorted(o.deps, key=lambda x: x.gseq):
                sk = self.sem_key(d)
                if clk.get(sk, 0) >= d.inc_val:
                    continue
                fw.append((sk, d.inc_val))
                clk[sk] = d.inc_val
                if d.snapshot:
                    for k2, v2 in d.snapshot.items():
                        if clk.get(k2, 0) < v2:
                            clk[k2] = v2
            o.final_waits = fw
            if o.need_inc:
                o.snapshot = dict(clk)
        self.sem_totals = counts

    def lower(self, block, sems):
        em = self

        def runner(ename):
            def run(eng):
                for o in em.eng_ops[ename]:
                    for (sk, val) in o.final_waits:
                        eng.wait_ge(sems[sk], val)
                    inst = o.fn(eng)
                    if o.need_inc:
                        inst.then_inc(sems[em.sem_key(o)], 16 if o.dma else 1)
                t = em.sem_totals.get((ename, "d"), 0)
                if t:
                    eng.wait_ge(sems[(ename, "d")], t)
            return run

        if self.eng_ops["sq"]:
            block.sync(runner("sq"))
        if self.eng_ops["pe"]:
            block.tensor(runner("pe"))
        if self.eng_ops["dve"]:
            block.vector(runner("dve"))
        if self.eng_ops["act"]:
            block.scalar(runner("act"))
        if self.eng_ops["gp"]:
            block.gpsimd(runner("gp"))


def R(t, lo=0, hi=1 << 30):
    return (t, lo, hi)


def build_program():
    nc = bass.Bass("TRN2", target_bir_lowering=False)
    em = Emitter()

    def din(name, shape, dt=F32):
        return nc.dram_tensor(name, shape, dt, kind="ExternalInput").ap()

    xo = din("xo", [TC, D])
    xp = din("xp", [TC, D])
    flag = din("flag", [128, 1])
    wq = din("wq", [D, DK]); wk = din("wk", [D, DK])
    wv = din("wv", [D, D]); wg = din("wg", [D, D]); wo_d = din("wo", [D, D])
    wgk1 = din("wgk1", [D, RG]); wgk2b = din("wgk2b", [RG + 1, DK])
    wgate = din("wgate", [D, IM]); wup = din("wup", [D, IM])
    wdown = din("wdown", [IM, D])
    moe_up = din("moe_up", [NE, 2 * EH, D])
    moe_dn = din("moe_dn", [NE, D, EH])
    router_rt = din("router_rt", [128, 8, 8])
    anorm = din("anorm", [128, 8]); mnorm = din("mnorm", [128, 8])
    fnorm = din("fnorm", [128, 8]); gnormc = din("gnormc", [128, 8])
    ident_in = din("ident", [128, 128], BF16)
    maskT_in = din("maskT", [64, 64])

    out_d = nc.dram_tensor("out", [TC, D], F32, kind="ExternalOutput").ap()

    scup = [nc.dram_tensor(f"scup{e}", [2 * EH, D], BF16, kind="Internal").ap()
            for e in range(NE)]
    scdn = [nc.dram_tensor(f"scdn{e}", [D, EH], BF16, kind="Internal").ap()
            for e in range(NE)]

    sb = {}

    def sball(name, shape, dt):
        sb[name] = nc.alloc_sbuf_tensor(name, shape, dt)
        return sb[name]

    scr = sball("scr", [128, 5, 1024], F32)
    ha = sball("ha", [128, 8, 1024], BF16)
    xf = sball("xf", [128, 8, 1024], BF16)
    qk = sball("qk", [128, 8, 1024], BF16)
    vt = sball("vt", [128, 8, 1024], BF16)
    sg = sball("sg", [128, 8, 1024], BF16)
    gl = sball("gl", [128, 4, 1024], BF16)
    ofl = sball("ofl", [128, 1, 1024], BF16)
    S = sball("S", [128, 4, 256], F32)
    Sb = sball("Sb", [128, 4, 256], BF16)
    act = sball("act", [128, 11, 1024], BF16)
    wdb = sball("wdb", [128, 11, 1024], BF16)
    wa = sball("wa", [128, 16384], BF16)
    t1b = sball("t1b", [128, 1024], BF16)
    mt = sball("mt", [128, 8, 8], F32)
    cst = sball("cst", [128, 4], F32)
    col = sball("col", [128, 80], F32)
    ident = sball("identt", [128, 128], BF16)
    maskT = sball("maskTs", [128, 64], F32)
    ncol = sball("ncol", [128, 32], F32)
    gscr = sball("gscr", [128, 2, 512], F32)
    gbf = sball("gbf", [128, 2, 1024], BF16)
    kdt = sball("kdt", [128, 2, 128], BF16)
    amb = sball("amb", [128, 2, 64], BF16)
    wgk1_bf = sball("wgk1b", [128, 8, 16], BF16)
    wgk2_s = sball("wgk2s", [128, 512], BF16)
    rb_s = sball("rbs", [128, 8, 8], BF16)

    pbf = [nc.alloc_psum_tensor(f"pbf{i}", [128, 512], F32) for i in range(6)]
    pbb = [nc.alloc_psum_tensor(f"pbb{i}", [128, 1024], BF16) for i in range(2)]
    PB = [R(f"pb{i}") for i in range(8)]

    def dma(queue, out_ap, in_ap, reads, writes, tag="", **kw):
        eng = {"gq": "gp", "aq": "act"}.get(queue, queue)
        em.op(eng, lambda e: e.dma_start(out=out_ap, in_=in_ap, **kw),
              reads=reads, writes=writes, dma=True, tag=tag)

    def mm(out_ap, lhsT, rhs, bank, reads, start, stop, tag="", tp=None):
        em.op("pe", lambda e: e.matmul(out_ap, lhsT, rhs, start=start,
                                       stop=stop, tile_position=tp),
              reads=reads, writes=[bank], tag=tag)

    def trans(out_ap, in_ap, bank, reads, tag="", tp=None):
        em.op("pe", lambda e: e.transpose(out_ap, in_ap, ident[:],
                                          tile_position=tp),
              reads=list(reads) + [R("identt")], writes=[bank], tag=tag)

    dma("sq", ident[:], ident_in, [R("d_ident")], [R("identt")])
    dma("sq", maskT[0:64, :], maskT_in, [R("d_maskT")], [R("maskTs")])
    dma("sq", maskT[64:128, :], maskT_in, [R("d_maskT")], [R("maskTs")])
    dma("sq", cst[:, 2:3], flag, [R("d_flag")], [R("cst", 2, 3)])
    dma("sq", ncol[:, 0:8], anorm, [R("d_anorm")], [R("ncol", 0, 8)])
    dma("sq", ncol[:, 8:16], mnorm, [R("d_mnorm")], [R("ncol", 8, 16)])
    dma("sq", ncol[:, 16:24], fnorm, [R("d_fnorm")], [R("ncol", 16, 24)])
    dma("sq", ncol[:, 24:32], gnormc, [R("d_gnorm")], [R("ncol", 24, 32)])
    em.op("dve", lambda e: e.memset(cst[:, 0:1], 1.0), writes=[R("cst", 0, 1)])
    em.op("dve", lambda e: e.memset(cst[:, 1:2], LNSCALE),
          writes=[R("cst", 1, 2)])
    em.op("dve", lambda e: e.memset(t1b[:], 1.0), writes=[R("t1b")])
    em.op("dve", lambda e: e.memset(S[:], 0.0), writes=[R("S")])
    dma("gq", wgk1_bf[:], wgk1.rearrange("(c p) r -> p c r", p=128),
        [R("d_wgk1")], [R("wgk1b")])
    dma("gq", wgk2_s[0:17, :], wgk2b, [R("d_wgk2b")], [R("wgk2s")])
    dma("gq", rb_s[:], router_rt, [R("d_router")], [R("rbs")])

    def cast_up(e):
        dstv = scup[e].rearrange("(pr g) d -> pr (g d)", pr=22, g=256)
        for two in range(2):
            dma("gq", dstv[:, two * 131072:(two + 1) * 131072],
                moe_up[e][two * EH:(two + 1) * EH, :].rearrange(
                    "(pr c) d -> pr (c d)", pr=22),
                [R("d_moeup")], [R(f"scup{e}")], tag=f"cup{e}.{two}")

    def cast_dn(e):
        dma("gq", scdn[e], moe_dn[e], [R("d_moedn")], [R(f"scdn{e}")],
            tag=f"cdn{e}")

    def ln_tile(x_rows, ti, blk, want_base, want_hn, tag):
        par = ti % 2
        cb = par * 16
        xs = 2
        xn_s = 3 * par
        sq_s = 4 - 3 * par
        x_t = scr[:, xs, :]
        xr = R("scr", xs * 1024, (xs + 1) * 1024)
        dma("sq", x_t, x_rows, [R("d_x")], [xr], tag=f"ldx{tag}")
        c = col

        def cr(i, j=None):
            j = i + 1 if j is None else j
            return R("col", cb + i, cb + j)

        def cc(i, j=None):
            return c[:, cb + i:cb + (i + 1 if j is None else j)]

        em.op("dve", lambda e: e.reduce_sum(cc(0), x_t, axis=AX.X),
              reads=[xr], writes=[cr(0)], tag=f"s1{tag}")
        sq_t = scr[:, sq_s, :]
        sqr = R("scr", sq_s * 1024, (sq_s + 1) * 1024)
        em.op("act", lambda e: e.activation(sq_t, x_t, func=AF.Square,
                                            accum_out=cc(1)),
              reads=[xr], writes=[sqr, cr(1)], tag=f"s2{tag}")
        em.op("dve", lambda e: e.tensor_scalar(cc(2), cc(0), 1.0 / D, None,
                                               op0=ALU.mult),
              reads=[cr(0)], writes=[cr(2)])
        em.op("dve", lambda e: e.tensor_scalar(cc(3), cc(1), 1.0 / D, 1e-5,
                                               op0=ALU.mult, op1=ALU.add),
              reads=[cr(1)], writes=[cr(3)])
        em.op("dve", lambda e: e.tensor_tensor(cc(4), cc(2), cc(2),
                                               op=ALU.mult),
              reads=[cr(2)], writes=[cr(4)])
        em.op("dve", lambda e: e.tensor_tensor(cc(5), cc(3), cc(4),
                                               op=ALU.subtract),
              reads=[cr(3, 5)], writes=[cr(5)])
        em.op("act", lambda e: e.activation(cc(6), cc(5), func=AF.Sqrt),
              reads=[cr(5)], writes=[cr(6)])
        em.op("dve", lambda e: e.reciprocal(cc(7), cc(6)),
              reads=[cr(6)], writes=[cr(7)], tag=f"rstd{tag}")
        xn_t = scr[:, xn_s, :]
        xnr = R("scr", xn_s * 1024, (xn_s + 1) * 1024)
        em.op("dve", lambda e: e.tensor_scalar(xn_t, x_t, cc(2), cc(7),
                                               op0=ALU.subtract, op1=ALU.mult),
              reads=[xr, cr(2), cr(7)], writes=[xnr], tag=f"xn{tag}")
        if want_base:
            b_t = scr[:, sq_s, :]
            br = sqr
            em.op("dve",
                  lambda e: e.scalar_tensor_tensor(b_t, xn_t, 2.0, x_t,
                                                   op0=ALU.mult, op1=ALU.add),
                  reads=[xnr, xr], writes=[br], tag=f"base{tag}")
            row0 = blk * TB + ti * 128
            dma("aq", out_d[row0:row0 + 128, :], b_t,
                [br], [R("d_out", row0, row0 + 128)], tag=f"stb{tag}")
        if want_hn:
            em.op("act", lambda e: e.activation(x_t, xn_t, func=AF.Square,
                                                accum_out=cc(8)),
                  reads=[xnr], writes=[xr, cr(8)])
            em.op("dve", lambda e: e.tensor_scalar(cc(9), cc(8), 1.0 / D, 1e-6,
                                                   op0=ALU.mult, op1=ALU.add),
                  reads=[cr(8)], writes=[cr(9)])
            em.op("act", lambda e: e.activation(cc(10), cc(9), func=AF.Sqrt),
                  reads=[cr(9)], writes=[cr(10)])
            em.op("dve", lambda e: e.reciprocal(cc(11), cc(10)),
                  reads=[cr(10)], writes=[cr(11)])
            hr_v = gbf[:, par, :]
            hrr = R("gbf", par * 1024, (par + 1) * 1024)
            em.op("dve", lambda e: e.tensor_scalar(hr_v, xn_t, cc(11), None,
                                                   op0=ALU.mult),
                  reads=[xnr, cr(11)], writes=[hrr], tag=f"hr{tag}")
            pt = pbb[par]
            for ch in range(8):
                trans(pt[:, ch * 128:(ch + 1) * 128],
                      hr_v[:, ch * 128:(ch + 1) * 128],
                      PB[6 + par], [hrr], tag=f"T{tag}.{ch}")
            for ch in range(8):
                em.op("dve",
                      lambda e, ch=ch: e.tensor_scalar(
                          ha[:, ch, ti * 128:(ti + 1) * 128],
                          pt[:, ch * 128:(ch + 1) * 128],
                          ncol[:, ch:ch + 1], None, op0=ALU.mult),
                      reads=[PB[6 + par], R("ncol", ch, ch + 1)],
                      writes=[R("ha", ch * 1024 + ti * 128,
                                ch * 1024 + (ti + 1) * 128)],
                      tag=f"evha{tag}.{ch}")
                em.op("dve",
                      lambda e, ch=ch: e.tensor_scalar(
                          xf[:, ch, ti * 128:(ti + 1) * 128],
                          pt[:, ch * 128:(ch + 1) * 128],
                          ncol[:, 16 + ch:17 + ch], None, op0=ALU.mult),
                      reads=[PB[6 + par], R("ncol", 16 + ch, 17 + ch)],
                      writes=[R("xf", ch * 1024 + ti * 128,
                                ch * 1024 + (ti + 1) * 128)],
                      tag=f"evxf{tag}.{ch}")
        return xn_t, xnr

    def wa_view(off, chunks, width):
        return wa[:, off:off + chunks * width].rearrange(
            "p (c k) -> p c k", c=chunks)

    def load_w_cpk(dst_view, off, nelem, src, tag):
        dma("gq", dst_view, src.rearrange("(c p) k -> p c k", p=128),
            [R("d_w" + tag)], [R("wa", off, off + nelem)], tag=tag)

    WQ_OFF, WK_OFF, WV_OFF = 0, 4096, 8192
    WG_OFF, WO_OFF = 0, 8192

    def proj_fm(nsl, w_off, wview, tag, dst_sl_off):
        for s in range(nsl):
            for th in range(2):
                bank = (s * 2 + th) % 2
                p = pbf[bank]
                for c in range(8):
                    mm(p[:], wview[:, c, s * 128:(s + 1) * 128],
                       ha[:, c, th * 512:(th + 1) * 512], PB[bank],
                       [R("wa", w_off + c * (nsl * 128) + s * 128,
                          w_off + c * (nsl * 128) + (s + 1) * 128),
                        R("ha", c * 1024 + th * 512,
                          c * 1024 + (th + 1) * 512)],
                       start=(c == 0), stop=(c == 7), tag=f"{tag}{s}.{th}.{c}")
                so = dst_sl_off + s
                em.op("dve",
                      lambda e, p=p, so=so, th=th: e.tensor_copy(
                          qk[:, so, th * 512:(th + 1) * 512], p[:]),
                      reads=[PB[bank]],
                      writes=[R("qk", so * 1024 + th * 512,
                                so * 1024 + (th + 1) * 512)],
                      tag=f"ev{tag}{s}.{th}")

    def proj_tm(dst_name, dst, w_off, wview, tag):
        for ti in range(NT):
            for dh in range(2):
                bank = (ti * 2 + dh) % 2
                p = pbf[bank]
                for c in range(8):
                    mm(p[:], ha[:, c, ti * 128:(ti + 1) * 128],
                       wview[:, c, dh * 512:(dh + 1) * 512], PB[bank],
                       [R("ha", c * 1024 + ti * 128,
                          c * 1024 + (ti + 1) * 128),
                        R("wa", w_off + c * 1024 + dh * 512,
                          w_off + c * 1024 + (dh + 1) * 512)],
                       start=(c == 0), stop=(c == 7), tag=f"{tag}{ti}.{dh}.{c}")
                em.op("dve",
                      lambda e, p=p, ti=ti, dh=dh: e.tensor_copy(
                          dst[:, ti, dh * 512:(dh + 1) * 512], p[:]),
                      reads=[PB[bank]],
                      writes=[R(dst_name, ti * 1024 + dh * 512,
                                ti * 1024 + (dh + 1) * 512)],
                      tag=f"ev{tag}{ti}.{dh}")

    def gate_path(tag):
        GS = _GSTEP
        for th in range(2):
            p = pbf[4]
            for c in range(8):
                mm(p[0:16, :], wgk1_bf[:, c, :],
                   ha[:, c, th * 512:(th + 1) * 512], PB[4],
                   [R("wgk1b"), R("ha", c * 1024 + th * 512,
                                  c * 1024 + (th + 1) * 512)],
                   start=(c == 0), stop=(c == 7), tag=f"t1{tag}{th}.{c}")
            em.op("dve",
                  lambda e, p=p, th=th: e.tensor_copy(
                      t1b[0:16, th * 512:(th + 1) * 512], p[0:16, :]),
                  reads=[PB[4]], writes=[R("t1b", th * 512, (th + 1) * 512)],
                  tag=f"evt1{tag}{th}")
        if GS < 2:
            return
        for gc in range(4):
            for th in range(2):
                par = (gc * 2 + th) % 2
                bank = 4 + par
                p = pbf[bank]
                mm(p[:], wgk2_s[0:17, gc * 128:(gc + 1) * 128],
                   t1b[0:17, th * 512:(th + 1) * 512], PB[bank],
                   [R("wgk2s"), R("t1b", th * 512, (th + 1) * 512)],
                   start=True, stop=True, tag=f"gp{tag}{gc}.{th}")
                a1 = gscr[:, par, 0:512]
                a1r = R("gscr", par * 512, par * 512 + 512)
                m0 = gbf[:, par, 0:512]
                m0r = R("gbf", par * 1024, par * 1024 + 512)
                if GS < 3:
                    em.op("dve", lambda e, p=p, gc=gc, th=th: e.tensor_scalar(
                              gl[:, gc, th * 512:(th + 1) * 512], p[:],
                              -1.0 / GLN, None, op0=ALU.mult),
                          reads=[PB[bank]],
                          writes=[R("gl", gc * 1024 + th * 512,
                                    gc * 1024 + (th + 1) * 512)],
                          tag=f"gl{tag}")
                    continue
                em.op("act", lambda e, p=p, a1=a1: e.activation(
                          a1, p[:], func=(AF.Square if GS == 32 else AF.Abs)),
                      reads=[PB[bank]], writes=[a1r], tag=f"abs{tag}")
                if GS == 31:
                    em.op("dve", lambda e, p=p, m0=m0: e.tensor_copy(m0, p[:]),
                          reads=[PB[bank]], writes=[m0r], tag=f"min{tag}")
                else:
                    em.op("dve", lambda e, p=p, m0=m0: e.tensor_scalar(
                              m0, p[:], 0.0, None, op0=ALU.min),
                          reads=[PB[bank]], writes=[m0r], tag=f"min{tag}")
                if GS < 4:
                    em.op("dve", lambda e, a1=a1, gc=gc, th=th: e.tensor_scalar(
                              gl[:, gc, th * 512:(th + 1) * 512], a1,
                              -1.0 / GLN, None, op0=ALU.mult),
                          reads=[a1r],
                          writes=[R("gl", gc * 1024 + th * 512,
                                    gc * 1024 + (th + 1) * 512)],
                          tag=f"gl{tag}")
                    continue
                em.op("act", lambda e, a1=a1: e.activation(
                          a1, a1, func=AF.Exp, scale=-1.0),
                      reads=[a1r], writes=[a1r], tag=f"exn{tag}")
                em.op("act", lambda e, a1=a1: e.activation(
                          a1, a1, func=AF.Ln, bias=cst[:, 0:1]),
                      reads=[a1r, R("cst", 0, 1)], writes=[a1r],
                      tag=f"ln{tag}")
                if GS < 5:
                    em.op("dve", lambda e, a1=a1, gc=gc, th=th: e.tensor_scalar(
                              gl[:, gc, th * 512:(th + 1) * 512], a1,
                              -1.0 / GLN, None, op0=ALU.mult),
                          reads=[a1r],
                          writes=[R("gl", gc * 1024 + th * 512,
                                    gc * 1024 + (th + 1) * 512)],
                          tag=f"gl{tag}")
                    continue
                em.op("dve", lambda e, a1=a1, m0=m0: e.tensor_tensor(
                          a1, a1, m0, op=ALU.subtract),
                      reads=[a1r, m0r], writes=[a1r], tag=f"sub{tag}")
                em.op("dve",
                      lambda e, a1=a1, gc=gc, th=th: e.tensor_scalar(
                          gl[:, gc, th * 512:(th + 1) * 512], a1,
                          -1.0 / GLN, None, op0=ALU.mult),
                      reads=[a1r],
                      writes=[R("gl", gc * 1024 + th * 512,
                                gc * 1024 + (th + 1) * 512)],
                      tag=f"gl{tag}")

    def router(blk):
        for ti in range(NT):
            p = pbf[4]
            for c in range(8):
                mm(p[:, 0:8], xf[:, c, ti * 128:(ti + 1) * 128],
                   rb_s[:, c, :], PB[4],
                   [R("xf", c * 1024 + ti * 128, c * 1024 + (ti + 1) * 128),
                    R("rbs")],
                   start=(c == 0), stop=(c == 7), tag=f"rt{blk}.{ti}.{c}")
            cb = (ti % 2) * 40

            def cc(i, j=None):
                return col[:, cb + i:cb + (i + 1 if j is None else j)]

            def cr(i, j=None):
                return R("col", cb + i, cb + (i + 1 if j is None else j))

            em.op("dve", lambda e, p=p, cc=cc: e.tensor_copy(cc(0, 8),
                                                             p[:, 0:8]),
                  reads=[PB[4]], writes=[cr(0, 8)], tag=f"sc{blk}.{ti}")
            em.op("dve", lambda e, cc=cc: e.reduce_max(cc(8), cc(0, 8),
                                                       axis=AX.X),
                  reads=[cr(0, 8)], writes=[cr(8)])
            em.op("dve", lambda e, cc=cc: e.tensor_scalar(
                      cc(9), cc(8), -1.0, None, op0=ALU.mult),
                  reads=[cr(8)], writes=[cr(9)])
            em.op("act", lambda e, cc=cc: e.activation(
                      cc(10, 18), cc(0, 8), func=AF.Exp, bias=cc(9)),
                  reads=[cr(0, 8), cr(9)], writes=[cr(10, 18)],
                  tag=f"sme{blk}.{ti}")
            em.op("dve", lambda e, cc=cc: e.reduce_sum(cc(18), cc(10, 18),
                                                       axis=AX.X),
                  reads=[cr(10, 18)], writes=[cr(18)])
            em.op("dve", lambda e, cc=cc: e.reciprocal(cc(19), cc(18)),
                  reads=[cr(18)], writes=[cr(19)])
            em.op("dve", lambda e, cc=cc: e.tensor_scalar(
                      cc(20, 28), cc(0, 8), cc(8), None, op0=ALU.is_ge),
                  reads=[cr(0, 8), cr(8)], writes=[cr(20, 28)])
            em.op("dve", lambda e, cc=cc: e.scalar_tensor_tensor(
                      cc(20, 28), cc(20, 28), -1e30, cc(0, 8),
                      op0=ALU.mult, op1=ALU.add),
                  reads=[cr(20, 28), cr(0, 8)], writes=[cr(20, 28)],
                  tag=f"mask{blk}.{ti}")
            em.op("dve", lambda e, cc=cc: e.reduce_max(cc(28), cc(20, 28),
                                                       axis=AX.X),
                  reads=[cr(20, 28)], writes=[cr(28)])
            em.op("dve", lambda e, cc=cc: e.tensor_scalar(
                      cc(20, 28), cc(0, 8), cc(28), None, op0=ALU.is_ge),
                  reads=[cr(0, 8), cr(28)], writes=[cr(20, 28)])
            em.op("dve", lambda e, cc=cc, ti=ti: e.scalar_tensor_tensor(
                      mt[:, ti, :], cc(10, 18), cc(19), cc(20, 28),
                      op0=ALU.mult, op1=ALU.mult),
                  reads=[cr(10, 18), cr(19), cr(20, 28)],
                  writes=[R("mt", ti * 8, (ti + 1) * 8)],
                  tag=f"mfin{blk}.{ti}")

    gcnt = [0]

    def gla_chunk(tg, ci, full):
        ti, half = ci // 2, ci % 2
        po = 64 * half
        for h in range(4):
            par = gcnt[0] % 2
            gcnt[0] += 1
            X = pbf[2 + par]
            Y = pbf[4 + par] if full else None
            KB = 6 + par
            gs = gscr[:, par, :]

            def gsr(a, b):
                return R("gscr", par * 512 + a, par * 512 + b)

            gb = gbf[:, par, :]

            def gbr(a, b):
                return R("gbf", par * 1024 + a, par * 1024 + b)

            glr = R("gl", h * 1024 + ci * 64, h * 1024 + ci * 64 + 64)
            b_t = gs[:, 0:64]
            em.op("dve", lambda e, b_t=b_t, h=h, ci=ci: e.tensor_tensor_scan(
                      b_t, gl[:, h, ci * 64:(ci + 1) * 64],
                      gl[:, h, ci * 64:(ci + 1) * 64], 0.0,
                      op0=ALU.add, op1=ALU.bypass),
                  reads=[glr], writes=[gsr(0, 64)], tag=f"scan{tg}")
            ebn = gb[:, 64:128]
            em.op("act", lambda e, ebn=ebn, b_t=b_t: e.activation(
                      ebn, b_t, func=AF.Exp, scale=-1.0),
                  reads=[gsr(0, 64)], writes=[gbr(64, 128)], tag=f"ebn{tg}")
            ebl = gs[:, 64:65]
            em.op("act", lambda e, ebl=ebl, b_t=b_t: e.activation(
                      ebl, b_t[:, 63:64], func=AF.Exp),
                  reads=[gsr(0, 64)], writes=[gsr(64, 65)], tag=f"ebl{tg}")
            kb_t = gb[:, 192:256]
            em.op("dve", lambda e, kb_t=kb_t, ebn=ebn, h=h, ci=ci:
                      e.tensor_tensor(kb_t,
                                      qk[:, 4 + h, ci * 64:(ci + 1) * 64],
                                      ebn, op=ALU.mult),
                  reads=[R("qk", (4 + h) * 1024 + ci * 64,
                           (4 + h) * 1024 + ci * 64 + 64), gbr(64, 128)],
                  writes=[gbr(192, 256)], tag=f"kb{tg}")
            kd_t = gb[:, 256:320]
            em.op("dve", lambda e, kd_t=kd_t, kb_t=kb_t, ebl=ebl:
                      e.tensor_scalar(kd_t, kb_t, ebl, None, op0=ALU.mult),
                  reads=[gbr(192, 256), gsr(64, 65)],
                  writes=[gbr(256, 320)], tag=f"kd{tg}")
            if full:
                ebq = gb[:, 0:64]
                em.op("act", lambda e, ebq=ebq, b_t=b_t: e.activation(
                          ebq, b_t, func=AF.Exp, bias=cst[:, 1:2]),
                      reads=[gsr(0, 64), R("cst", 1, 2)],
                      writes=[gbr(0, 64)], tag=f"ebq{tg}")
                qe_t = gb[:, 128:192]
                em.op("dve", lambda e, qe_t=qe_t, ebq=ebq, h=h, ci=ci:
                          e.tensor_tensor(qe_t,
                                          qk[:, h, ci * 64:(ci + 1) * 64],
                                          ebq, op=ALU.mult),
                      reads=[R("qk", h * 1024 + ci * 64,
                               h * 1024 + ci * 64 + 64), gbr(0, 64)],
                      writes=[gbr(128, 192)], tag=f"qe{tg}")
                em.op("pe", lambda e, X=X, kb_t=kb_t, qe_t=qe_t:
                          e.matmul(X[po:po + 64, 0:64], kb_t, qe_t,
                                   start=True, stop=True,
                                   tile_position=(0, po)),
                      reads=[gbr(192, 256), gbr(128, 192)],
                      writes=[PB[2 + par]], tag=f"mmA{tg}")
                am_t = amb[po:po + 64, par, :]
                em.op("dve", lambda e, am_t=am_t, X=X: e.tensor_tensor(
                          am_t, X[po:po + 64, 0:64], maskT[po:po + 64, :],
                          op=ALU.mult),
                      reads=[PB[2 + par], R("maskTs")],
                      writes=[R("amb", par * 64, par * 64 + 64)],
                      tag=f"am{tg}")
                em.op("pe", lambda e, Y=Y, am_t=am_t, ti=ti, h=h:
                          e.matmul(Y[po:po + 64, 0:256], am_t,
                                   vt[po:po + 64, ti, h * 256:(h + 1) * 256],
                                   start=True, stop=False,
                                   tile_position=(po, po)),
                      reads=[R("amb", par * 64, par * 64 + 64),
                             R("vt", ti * 1024 + h * 256,
                               ti * 1024 + (h + 1) * 256)],
                      writes=[PB[4 + par]], tag=f"mmoi{tg}")
                em.op("pe", lambda e, Y=Y, qe_t=qe_t, h=h:
                          e.matmul(Y[po:po + 64, 0:256], qe_t, Sb[:, h, :],
                                   start=False, stop=True,
                                   tile_position=(0, po)),
                      reads=[gbr(128, 192), R("Sb", h * 256, (h + 1) * 256)],
                      writes=[PB[4 + par]], tag=f"mmos{tg}")
            em.op("pe", lambda e, kd_t=kd_t, par=par: e.transpose(
                      pbb[par][po:po + 64, 0:128], kd_t, ident[:],
                      tile_position=(0, po)),
                  reads=[gbr(256, 320), R("identt")], writes=[PB[KB]],
                  tag=f"Tkd{tg}")
            kdt_t = kdt[po:po + 64, par, :]
            em.op("dve", lambda e, kdt_t=kdt_t, par=par: e.tensor_copy(
                      kdt_t, pbb[par][po:po + 64, 0:128]),
                  reads=[PB[KB]], writes=[R("kdt", par * 128, par * 128 + 128)],
                  tag=f"evkd{tg}")
            em.op("pe", lambda e, X=X, kdt_t=kdt_t, ti=ti, h=h:
                      e.matmul(X[:, 0:256], kdt_t,
                               vt[po:po + 64, ti, h * 256:(h + 1) * 256],
                               start=True, stop=True, tile_position=(po, 0)),
                  reads=[R("kdt", par * 128, par * 128 + 128),
                         R("vt", ti * 1024 + h * 256,
                           ti * 1024 + (h + 1) * 256)],
                  writes=[PB[2 + par]], tag=f"mmS{tg}")
            em.op("dve", lambda e, X=X, ebl=ebl, h=h:
                      e.scalar_tensor_tensor(S[:, h, :], S[:, h, :], ebl,
                                             X[:, 0:256], op0=ALU.mult,
                                             op1=ALU.add),
                  reads=[R("S", h * 256, (h + 1) * 256), gsr(64, 65),
                         PB[2 + par]],
                  writes=[R("S", h * 256, (h + 1) * 256)], tag=f"sup{tg}")
            if full:
                em.op("act", lambda e, h=h: e.activation(
                          Sb[:, h, :], S[:, h, :], func=AF.Copy),
                      reads=[R("S", h * 256, (h + 1) * 256)],
                      writes=[R("Sb", h * 256, (h + 1) * 256)], tag=f"sbf{tg}")
                sqo = gs[po:po + 64, 128:384]
                em.op("act", lambda e, sqo=sqo, Y=Y: e.activation(
                          sqo, Y[po:po + 64, 0:256], func=AF.Square),
                      reads=[PB[4 + par]], writes=[gsr(128, 384)],
                      tag=f"sqo{tg}")
                em.op("dve", lambda e, sqo=sqo, gs=gs: e.reduce_sum(
                          gs[po:po + 64, 384:385], sqo, axis=AX.X),
                      reads=[gsr(128, 384)], writes=[gsr(384, 385)],
                      tag=f"sqr{tg}")
                em.op("dve", lambda e, gs=gs: e.tensor_scalar(
                          gs[po:po + 64, 385:386], gs[po:po + 64, 384:385],
                          1.0 / DV, 1e-6, op0=ALU.mult, op1=ALU.add),
                      reads=[gsr(384, 385)], writes=[gsr(385, 386)])
                em.op("act", lambda e, gs=gs: e.activation(
                          gs[po:po + 64, 386:387], gs[po:po + 64, 385:386],
                          func=AF.Ln),
                      reads=[gsr(385, 386)], writes=[gsr(386, 387)])
                em.op("act", lambda e, gs=gs: e.activation(
                          gs[po:po + 64, 387:388], gs[po:po + 64, 386:387],
                          func=AF.Exp, scale=-0.5),
                      reads=[gsr(386, 387)], writes=[gsr(387, 388)])
                em.op("dve", lambda e, Y=Y, gs=gs, ti=ti, h=h:
                          e.scalar_tensor_tensor(
                              ofl[po:po + 64, 0, h * 256:(h + 1) * 256],
                              Y[po:po + 64, 0:256],
                              gs[po:po + 64, 387:388],
                              sg[po:po + 64, ti, h * 256:(h + 1) * 256],
                              op0=ALU.mult, op1=ALU.mult),
                      reads=[PB[4 + par], gsr(387, 388),
                             R("sg", ti * 1024 + h * 256,
                               ti * 1024 + (h + 1) * 256)],
                      writes=[R("ofl", h * 256, (h + 1) * 256)],
                      tag=f"on{tg}")
        if full and half == 1:
            tp_ = ti % 2
            for ch in range(8):
                trans(pbb[tp_][:, ch * 128:(ch + 1) * 128],
                      ofl[:, 0, ch * 128:(ch + 1) * 128], PB[6 + tp_],
                      [R("ofl", ch * 128, (ch + 1) * 128)],
                      tag=f"To{tg}.{ch}")
            for ch in range(8):
                em.op("dve", lambda e, ch=ch, ti=ti, tp_=tp_: e.tensor_scalar(
                          ha[:, ch, ti * 128:(ti + 1) * 128],
                          pbb[tp_][:, ch * 128:(ch + 1) * 128],
                          ncol[:, 24 + ch:25 + ch], None, op0=ALU.mult),
                      reads=[PB[6 + tp_], R("ncol", 24 + ch, 25 + ch)],
                      writes=[R("ha", ch * 1024 + ti * 128,
                                ch * 1024 + (ti + 1) * 128)],
                      tag=f"evo{tg}.{ch}")

    def down_half(t, half, rhs_name, rhs, e_i=None):
        for ti in range(NT):
            par = ti % 2
            st_s = 1 + 3 * par
            stg = scr[:, st_s, :]
            for dh in range(2):
                bank = 4 + ((ti * 2 + dh) % 2)
                p = pbf[bank]
                for j in range(11):
                    mm(p[:], act[:, j, ti * 128:(ti + 1) * 128],
                       rhs[:, j, dh * 512:(dh + 1) * 512], PB[bank],
                       [R("act", j * 1024 + ti * 128,
                          j * 1024 + (ti + 1) * 128),
                        R(rhs_name, j * 1024 + dh * 512,
                          j * 1024 + (dh + 1) * 512)],
                       start=(j == 0), stop=(j == 10),
                       tag=f"dn{t}.{half}.{ti}.{dh}.{j}")
                if e_i is None:
                    em.op("dve", lambda e, p=p, stg=stg, dh=dh: e.tensor_copy(
                              stg[:, dh * 512:(dh + 1) * 512], p[:]),
                          reads=[PB[bank]],
                          writes=[R("scr", st_s * 1024 + dh * 512,
                                    st_s * 1024 + (dh + 1) * 512)],
                          tag=f"dnev{t}.{half}.{ti}.{dh}")
                else:
                    em.op("dve",
                          lambda e, p=p, stg=stg, dh=dh, ti=ti, e_i=e_i:
                              e.tensor_scalar(
                                  stg[:, dh * 512:(dh + 1) * 512], p[:],
                                  mt[:, ti, e_i:e_i + 1], None,
                                  op0=ALU.mult),
                          reads=[PB[bank], R("mt", ti * 8 + e_i,
                                             ti * 8 + e_i + 1)],
                          writes=[R("scr", st_s * 1024 + dh * 512,
                                    st_s * 1024 + (dh + 1) * 512)],
                          tag=f"dnev{t}.{half}.{ti}.{dh}")
            blk = int(t[1])
            row0 = blk * TB + ti * 128
            dma("gq", out_d[row0:row0 + 128, :], stg,
                [R("scr", st_s * 1024, (st_s + 1) * 1024)],
                [R("d_out", row0, row0 + 128)], accum_op=ALU.add,
                tag=f"dnst{t}.{half}.{ti}")

    PH = _PHASES
    if PH < 1:
        return _finish(nc, em)
    load_w_cpk(wa_view(WK_OFF, 8, 512), WK_OFF, 4096, wk, "k0")
    load_w_cpk(wa_view(WV_OFF, 8, 1024), WV_OFF, 8192, wv, "v0")
    for e in range(4):
        cast_up(e)
    for sblk in range(2):
        for ti in range(NT):
            ln_tile(xp[sblk * TB + ti * 128:sblk * TB + (ti + 1) * 128, :],
                    ti, sblk, want_base=False, want_hn=True,
                    tag=f"sp{sblk}t{ti}")
        if _SPH < 1:
            continue
        proj_fm(4, WK_OFF, wa_view(WK_OFF, 8, 512), f"spk{sblk}",
                dst_sl_off=4)
        if _SPH < 2:
            continue
        proj_tm("vt", vt, WV_OFF, wa_view(WV_OFF, 8, 1024), f"spv{sblk}")
        if _SPH < 3:
            continue
        gate_path(f"sg{sblk}")
        if _SPH < 4:
            continue
        for ci in range(NCHUNK):
            gla_chunk(f"sp{sblk}c{ci}", ci, full=False)
    for h in range(4):
        em.op("dve", lambda e, h=h: e.tensor_scalar(
                  S[:, h, :], S[:, h, :], cst[:, 2:3], None, op0=ALU.mult),
              reads=[R("S", h * 256, (h + 1) * 256), R("cst", 2, 3)],
              writes=[R("S", h * 256, (h + 1) * 256)], tag=f"flag{h}")
        em.op("act", lambda e, h=h: e.activation(
                  Sb[:, h, :], S[:, h, :], func=AF.Copy),
              reads=[R("S", h * 256, (h + 1) * 256)],
              writes=[R("Sb", h * 256, (h + 1) * 256)], tag=f"sbflag{h}")

    if PH < 2:
        return _finish(nc, em)
    for blk in range(2):
        t = f"b{blk}"
        load_w_cpk(wa_view(WQ_OFF, 8, 512), WQ_OFF, 4096, wq, f"q{blk}")
        if blk == 0:
            for e in range(4, 8):
                cast_up(e)
        if blk == 1:
            load_w_cpk(wa_view(WK_OFF, 8, 512), WK_OFF, 4096, wk, f"k{blk}")
            load_w_cpk(wa_view(WV_OFF, 8, 1024), WV_OFF, 8192, wv, f"v{blk}")
        for ti in range(NT):
            ln_tile(xo[blk * TB + ti * 128:blk * TB + (ti + 1) * 128, :],
                    ti, blk, want_base=True, want_hn=True, tag=f"{t}t{ti}")
        if PH < 3:
            continue
        proj_fm(4, WQ_OFF, wa_view(WQ_OFF, 8, 512), f"pq{blk}", dst_sl_off=0)
        proj_fm(4, WK_OFF, wa_view(WK_OFF, 8, 512), f"pk{blk}", dst_sl_off=4)
        proj_tm("vt", vt, WV_OFF, wa_view(WV_OFF, 8, 1024), f"pv{blk}")
        load_w_cpk(wa_view(WG_OFF, 8, 1024), WG_OFF, 8192, wg, f"g{blk}")
        if blk == 0:
            for e in range(4):
                cast_dn(e)
        proj_tm("sg", sg, WG_OFF, wa_view(WG_OFF, 8, 1024), f"pg{blk}")
        for ti in range(NT):
            em.op("act", lambda e, ti=ti: e.activation(
                      sg[:, ti, :], sg[:, ti, :], func=AF.Silu),
                  reads=[R("sg", ti * 1024, (ti + 1) * 1024)],
                  writes=[R("sg", ti * 1024, (ti + 1) * 1024)],
                  tag=f"silu{blk}.{ti}")
        gate_path(f"g{blk}")
        router(blk)
        load_w_cpk(wa_view(WO_OFF, 8, 1024), WO_OFF, 8192, wo_d, f"o{blk}")
        if blk == 0:
            for e in range(4, 8):
                cast_dn(e)
        if PH < 4:
            continue
        for ci in range(NCHUNK):
            gla_chunk(f"{t}c{ci}", ci, full=True)

        if PH < 5:
            continue
        wo_v = wa_view(WO_OFF, 8, 1024)
        for ti in range(NT):
            par = ti % 2
            xn_t, xnr = ln_tile(
                xo[blk * TB + ti * 128:blk * TB + (ti + 1) * 128, :],
                ti, blk, want_base=False, want_hn=False, tag=f"r{t}t{ti}")
            st_s = 1 + 3 * par
            stg = scr[:, st_s, :]
            x2_s = 4 - 3 * par
            x2_t = scr[:, x2_s, :]
            x2r = R("scr", x2_s * 1024, (x2_s + 1) * 1024)
            for dh in range(2):
                bank = dh
                p = pbf[bank]
                for c in range(8):
                    mm(p[:], ha[:, c, ti * 128:(ti + 1) * 128],
                       wo_v[:, c, dh * 512:(dh + 1) * 512], PB[bank],
                       [R("ha", c * 1024 + ti * 128,
                          c * 1024 + (ti + 1) * 128),
                        R("wa", WO_OFF + c * 1024 + dh * 512,
                          WO_OFF + c * 1024 + (dh + 1) * 512)],
                       start=(c == 0), stop=(c == 7),
                       tag=f"wo{t}.{ti}.{dh}.{c}")
                em.op("dve", lambda e, p=p, stg=stg, dh=dh: e.tensor_copy(
                          stg[:, dh * 512:(dh + 1) * 512], p[:]),
                      reads=[PB[bank]],
                      writes=[R("scr", st_s * 1024 + dh * 512,
                                st_s * 1024 + (dh + 1) * 512)],
                      tag=f"evst{t}.{ti}.{dh}")
                em.op("dve", lambda e, p=p, x2_t=x2_t, xn_t=xn_t, dh=dh:
                          e.tensor_tensor(x2_t[:, dh * 512:(dh + 1) * 512],
                                          p[:],
                                          xn_t[:, dh * 512:(dh + 1) * 512],
                                          op=ALU.add),
                      reads=[PB[bank], xnr],
                      writes=[R("scr", x2_s * 1024 + dh * 512,
                                x2_s * 1024 + (dh + 1) * 512)],
                      tag=f"evx2{t}.{ti}.{dh}")
            row0 = blk * TB + ti * 128
            dma("gq", out_d[row0:row0 + 128, :], stg,
                [R("scr", st_s * 1024, (st_s + 1) * 1024)],
                [R("d_out", row0, row0 + 128)], accum_op=ALU.add,
                tag=f"stat{t}.{ti}")
            cb = par * 16

            def cc(i):
                return col[:, cb + i:cb + i + 1]

            def crr(i):
                return R("col", cb + i, cb + i + 1)

            sqt2 = scr[:, 2, :]
            sqr2 = R("scr", 2 * 1024, 3 * 1024)
            em.op("act", lambda e, sqt2=sqt2, x2_t=x2_t, cc=cc: e.activation(
                      sqt2, x2_t, func=AF.Square, accum_out=cc(12)),
                  reads=[x2r], writes=[sqr2, crr(12)], tag=f"n2sq{t}.{ti}")
            em.op("dve", lambda e, cc=cc: e.tensor_scalar(
                      cc(13), cc(12), 1.0 / D, 1e-6, op0=ALU.mult,
                      op1=ALU.add),
                  reads=[crr(12)], writes=[crr(13)])
            em.op("act", lambda e, cc=cc: e.activation(cc(14), cc(13),
                                                       func=AF.Sqrt),
                  reads=[crr(13)], writes=[crr(14)])
            em.op("dve", lambda e, cc=cc: e.reciprocal(cc(15), cc(14)),
                  reads=[crr(14)], writes=[crr(15)])
            n2r_v = gbf[:, par, :]
            n2rr = R("gbf", par * 1024, (par + 1) * 1024)
            em.op("dve", lambda e, n2r_v=n2r_v, x2_t=x2_t, cc=cc:
                      e.tensor_scalar(n2r_v, x2_t, cc(15), None,
                                      op0=ALU.mult),
                  reads=[x2r, crr(15)], writes=[n2rr], tag=f"n2r{t}.{ti}")
            for ch in range(8):
                trans(pbb[par][:, ch * 128:(ch + 1) * 128],
                      n2r_v[:, ch * 128:(ch + 1) * 128], PB[6 + par],
                      [n2rr], tag=f"Tn2{t}.{ti}.{ch}")
            for ch in range(8):
                em.op("dve", lambda e, ch=ch, ti=ti, par=par: e.tensor_scalar(
                          qk[:, ch, ti * 128:(ti + 1) * 128],
                          pbb[par][:, ch * 128:(ch + 1) * 128],
                          ncol[:, 8 + ch:9 + ch], None, op0=ALU.mult),
                      reads=[PB[6 + par], R("ncol", 8 + ch, 9 + ch)],
                      writes=[R("qk", ch * 1024 + ti * 128,
                                ch * 1024 + (ti + 1) * 128)],
                      tag=f"evn2{t}.{ti}.{ch}")

        def mlp_updown():
            gpair = 0
            mlp_ranges = [(i, min(512, IM - i)) for i in range(0, IM, 512)]
            for (i0, w) in mlp_ranges:
                rpar = (i0 // 512) % 2
                go = 8192 * rpar
                uo = 8192 * rpar + 4096
                dma("gq", wa_view(go, 8, 512)[:, :, 0:w],
                    wgate[:, i0:i0 + w].rearrange("(c p) i -> p c i", p=128),
                    [R("d_wgate")], [R("wa", go, go + 4096)],
                    tag=f"lwg{t}.{i0}")
                dma("gq", wa_view(uo, 8, 512)[:, :, 0:w],
                    wup[:, i0:i0 + w].rearrange("(c p) i -> p c i", p=128),
                    [R("d_wup")], [R("wa", uo, uo + 4096)],
                    tag=f"lwu{t}.{i0}")
                for ps in range(w // 128):
                    half, j = gpair // 11, gpair % 11
                    gp2 = gpair % 2
                    for th in range(2):
                        pgb = pbf[0 + gp2]
                        pub = pbf[2 + gp2]
                        for c in range(8):
                            mm(pgb[:],
                               wa_view(go, 8, 512)[:, c,
                                                   ps * 128:(ps + 1) * 128],
                               qk[:, c, th * 512:(th + 1) * 512], PB[0 + gp2],
                               [R("wa", go + c * 512 + ps * 128,
                                  go + c * 512 + (ps + 1) * 128),
                                R("qk", c * 1024 + th * 512,
                                  c * 1024 + (th + 1) * 512)],
                               start=(c == 0), stop=(c == 7),
                               tag=f"mlg{t}.{gpair}.{th}.{c}")
                        for c in range(8):
                            mm(pub[:],
                               wa_view(uo, 8, 512)[:, c,
                                                   ps * 128:(ps + 1) * 128],
                               qk[:, c, th * 512:(th + 1) * 512], PB[2 + gp2],
                               [R("wa", uo + c * 512 + ps * 128,
                                  uo + c * 512 + (ps + 1) * 128),
                                R("qk", c * 1024 + th * 512,
                                  c * 1024 + (th + 1) * 512)],
                               start=(c == 0), stop=(c == 7),
                               tag=f"mlu{t}.{gpair}.{th}.{c}")
                        sil = gbf[:, th, 0:512]
                        em.op("act", lambda e, sil=sil, pgb=pgb: e.activation(
                                  sil, pgb[:], func=AF.Silu),
                              reads=[PB[0 + gp2]],
                              writes=[R("gbf", th * 1024, th * 1024 + 512)],
                              tag=f"msil{t}.{gpair}.{th}")
                        em.op("dve",
                              lambda e, sil=sil, pub=pub, j=j, th=th:
                                  e.tensor_tensor(
                                      act[:, j, th * 512:(th + 1) * 512],
                                      pub[:], sil, op=ALU.mult),
                              reads=[PB[2 + gp2],
                                     R("gbf", th * 1024, th * 1024 + 512)],
                              writes=[R("act", j * 1024 + th * 512,
                                        j * 1024 + (th + 1) * 512)],
                              tag=f"mact{t}.{gpair}.{th}")
                    gpair += 1
                    if gpair % 11 == 0:
                        hf = gpair // 11 - 1
                        dma("gq", wdb[:],
                            wdown[hf * 1408:(hf + 1) * 1408, :].rearrange(
                                "(j p) d -> p j d", p=128),
                            [R("d_wdown")], [R("wdb")], tag=f"lwd{t}.{hf}")
                        down_half(t, hf, "wdb", wdb, e_i=None)

        if PH >= 6:
            mlp_updown()

        if PH < 7:
            continue
        for e_i in range(NE):
            scu = scup[e_i]
            scd = scdn[e_i]
            up_ranges = [(r0, min(1024, 2 * EH - r0))
                         for r0 in range(0, 2 * EH, 1024)]
            gpair = 0
            for (r0, rw) in up_ranges:
                rpar = (r0 // 1024) % 2
                so = 8192 * rpar
                for c in range(8):
                    dma("sq", wa_view(so, 8, 1024)[:, c, 0:rw],
                        scu[r0:r0 + rw, c * 128:(c + 1) * 128],
                        [R(f"scup{e_i}")],
                        [R("wa", so + c * 1024, so + c * 1024 + rw)],
                        transpose=True, tag=f"xu{t}.{e_i}.{r0}.{c}")
                npair = rw // 256
                for lp in range(npair):
                    half, j = gpair // 11, gpair % 11
                    gp2 = gpair % 2
                    for th in range(2):
                        pvb = pbf[0 + gp2]
                        pgb = pbf[2 + gp2]
                        for c in range(8):
                            mm(pvb[:],
                               wa_view(so, 8, 1024)[:, c,
                                                    lp * 256:lp * 256 + 128],
                               xf[:, c, th * 512:(th + 1) * 512], PB[0 + gp2],
                               [R("wa", so + c * 1024 + lp * 256,
                                  so + c * 1024 + lp * 256 + 128),
                                R("xf", c * 1024 + th * 512,
                                  c * 1024 + (th + 1) * 512)],
                               start=(c == 0), stop=(c == 7),
                               tag=f"mev{t}.{e_i}.{gpair}.{th}.{c}")
                        for c in range(8):
                            mm(pgb[:],
                               wa_view(so, 8, 1024)[:, c,
                                                    lp * 256 + 128:
                                                    lp * 256 + 256],
                               xf[:, c, th * 512:(th + 1) * 512], PB[2 + gp2],
                               [R("wa", so + c * 1024 + lp * 256 + 128,
                                  so + c * 1024 + lp * 256 + 256),
                                R("xf", c * 1024 + th * 512,
                                  c * 1024 + (th + 1) * 512)],
                               start=(c == 0), stop=(c == 7),
                               tag=f"meg{t}.{e_i}.{gpair}.{th}.{c}")
                        sil = gbf[:, th, 0:512]
                        em.op("act", lambda e, sil=sil, pgb=pgb: e.activation(
                                  sil, pgb[:], func=AF.Silu),
                              reads=[PB[2 + gp2]],
                              writes=[R("gbf", th * 1024, th * 1024 + 512)],
                              tag=f"esil{t}.{e_i}.{gpair}.{th}")
                        em.op("dve",
                              lambda e, sil=sil, pvb=pvb, j=j, th=th:
                                  e.tensor_tensor(
                                      act[:, j, th * 512:(th + 1) * 512],
                                      pvb[:], sil, op=ALU.mult),
                              reads=[PB[0 + gp2],
                                     R("gbf", th * 1024, th * 1024 + 512)],
                              writes=[R("act", j * 1024 + th * 512,
                                        j * 1024 + (th + 1) * 512)],
                              tag=f"eact{t}.{e_i}.{gpair}.{th}")
                    gpair += 1
                    if gpair % 11 == 0:
                        hf = gpair // 11 - 1
                        for j in range(11):
                            dma("sq", wdb[:, j, :],
                                scd[:, (11 * hf + j) * 128:
                                    (11 * hf + j + 1) * 128],
                                [R(f"scdn{e_i}")],
                                [R("wdb", j * 1024, (j + 1) * 1024)],
                                transpose=True, tag=f"xd{t}.{e_i}.{hf}.{j}")
                        down_half(t, hf, "wdb", wdb, e_i=e_i)

    for name in _DEBUG_TAPS:
        tns = sb[name]
        shp = list(tns[:].shape)
        n = int(np.prod(shp[1:]))
        o = nc.dram_tensor(f"tap_{name}", [shp[0], n], F32,
                           kind="ExternalOutput").ap()
        _DEBUG_SHAPES[name] = shp
        srcv = tns[:] if len(shp) == 2 else tns[:].rearrange("p a b -> p (a b)")
        dma("gq", o, srcv, [R(name)], [R(f"d_tap_{name}")])

    return _finish(nc, em)


def _finish(nc, em):
    em.finalize()
    with contextlib.ExitStack() as stack:
        sems = {}
        for k in em.sem_totals:
            sems[k] = stack.enter_context(nc.semaphore(f"sem_{k[0]}_{k[1]}"))
        blk_ = stack.enter_context(nc.Block())
        em.lower(blk_, sems)
    return nc



def kernel(**inputs):
    hid = np.ascontiguousarray(np.asarray(inputs["L0_attn_hidden_states"],
                                          dtype=np.float32)).reshape(-1, D)
    assert hid.shape[0] == 8 * TC

    def col8(v):
        return np.ascontiguousarray(np.asarray(v, np.float32).reshape(8, 128).T)

    shared = {
        "wq": np.ascontiguousarray(inputs["L0_attn_Wq"], np.float32),
        "wk": np.ascontiguousarray(inputs["L0_attn_Wk"], np.float32),
        "wv": np.ascontiguousarray(inputs["L0_attn_Wv"], np.float32),
        "wg": np.ascontiguousarray(inputs["L0_attn_Wg"], np.float32),
        "wo": np.ascontiguousarray(inputs["L0_attn_Wo"], np.float32),
        "wgk1": np.ascontiguousarray(inputs["L0_attn_Wgk1"], np.float32),
        "wgk2b": np.ascontiguousarray(
            np.concatenate([np.asarray(inputs["L0_attn_Wgk2"], np.float32),
                            np.asarray(inputs["L0_attn_bgk2"],
                                       np.float32)[None, :]], axis=0)),
        "wgate": np.ascontiguousarray(inputs["L0_attn_W_gate"], np.float32),
        "wup": np.ascontiguousarray(inputs["L0_attn_W_up"], np.float32),
        "wdown": np.ascontiguousarray(inputs["L0_attn_W_down"], np.float32),
        "moe_up": np.ascontiguousarray(inputs["L0_ffn_w_up"], np.float32),
        "moe_dn": np.ascontiguousarray(inputs["L0_ffn_w_down"], np.float32),
        "router_rt": np.ascontiguousarray(
            np.asarray(inputs["L0_ffn_w_router"], np.float32).T
            .reshape(8, 128, 8).transpose(1, 0, 2)),
        "anorm": col8(inputs["L0_attn_attn_norm_w"]),
        "mnorm": col8(inputs["L0_attn_mlp_norm_w"]),
        "fnorm": col8(inputs["L0_ffn_norm_scale"]),
        "gnormc": np.ascontiguousarray(
            np.tile(np.asarray(inputs["L0_attn_gnorm_w"], np.float32), NH)
            .reshape(8, 128).T),
        "ident": np.eye(128, dtype=ml_dtypes.bfloat16),
        "maskT": np.ascontiguousarray(np.triu(np.ones((64, 64), np.float32))),
    }

    zeros_x = np.zeros((TC, D), np.float32)
    in_maps = []
    for c in range(8):
        m = dict(shared)
        m["xo"] = np.ascontiguousarray(hid[c * TC:(c + 1) * TC])
        m["xp"] = (np.ascontiguousarray(hid[(c - 1) * TC:c * TC])
                   if c % 2 == 1 else zeros_x)
        m["flag"] = np.full((128, 1), 1.0 if c % 2 == 1 else 0.0, np.float32)
        in_maps.append(m)

    nc = build_program()
    res = run_bass_kernel_spmd(nc, in_maps, core_ids=list(range(8)))
    global _DEBUG_OUT
    _DEBUG_OUT = list(res.results)
    out = np.concatenate([r["out"] for r in res.results], axis=0)
    return out.reshape(4, 4096, D)
